# version 17
# baseline (speedup 1.0000x reference)
import numpy as np
import ml_dtypes
import concourse.bass as bass
import concourse.mybir as mybir
from concourse.bass_utils import run_bass_kernel_spmd

F32 = mybir.dt.float32
BF16 = mybir.dt.bfloat16
AF = mybir.ActivationFunctionType
ALU = mybir.AluOpType

NCORES = 8
NTOK = 8192
D = 1024
HID = 512
DFF = 4096
TILE = 512
GRP = 128
SGTOK = 1024
SCALE = 64.0 ** -0.5
EPS = 1e-5

ENGS = ("tensor", "vector", "scalar", "sync", "gpsimd")


class Build:

    def __init__(self, sems):
        self.sems = sems
        self.cnt = {k: 0 for k in sems}
        self.prog = {e: [] for e in ENGS}
        self.hwm = {e: {} for e in ENGS}

    def wait(self, eng, ev):
        if ev is None:
            return
        sem, val = ev
        if self.hwm[eng].get(sem, 0) >= val:
            return
        self.hwm[eng][sem] = val
        h = self.sems[sem]
        self.prog[eng].append(lambda e, h=h, v=val: e.wait_ge(h, v))

    def op(self, eng, f, waits=(), inc=None, inck=1):
        for ev in waits:
            self.wait(eng, ev)
        if inc is None:
            self.prog[eng].append(f)
            return None
        self.cnt[inc] += inck
        val = self.cnt[inc]
        h = self.sems[inc]
        self.prog[eng].append(lambda e, f=f, h=h, k=inck: f(e).then_inc(h, k))
        return (inc, val)


def build_nc():
    NIC = D // 128
    NMC_Q = HID // 128
    NKC = DFF // 128
    NTILES = NTOK // TILE
    NGRP = TILE // GRP
    NSG = NTOK // SGTOK
    INV_D = 1.0 / D
    NB12 = DFF // 256
    NBH = DFF // 128
    NB3 = D // 128

    nc = bass.Bass("TRN2", target_bir_lowering=False,
                   detect_race_conditions=False)

    xT_d = nc.dram_tensor("xT", [D, NTOK], F32, kind="ExternalInput")
    wq_d = nc.dram_tensor("wq", [D, HID], F32, kind="ExternalInput")
    wk_d = nc.dram_tensor("wk", [D, HID], F32, kind="ExternalInput")
    wv_d = nc.dram_tensor("wv", [D, HID], F32, kind="ExternalInput")
    wo_d = nc.dram_tensor("wo", [HID, D], F32, kind="ExternalInput")
    bocol_d = nc.dram_tensor("bocol", [128, NIC], F32, kind="ExternalInput")
    w1t_d = nc.dram_tensor("w1t", [NB12, 128, NIC * 256], F32, kind="ExternalInput")
    w2t_d = nc.dram_tensor("w2t", [NB12, 128, NIC * 256], F32, kind="ExternalInput")
    h1t_d = nc.dram_tensor("h1t", [NBH, 128, NKC * 128], F32, kind="ExternalInput")
    h2t_d = nc.dram_tensor("h2t", [NBH, 128, NKC * 128], F32, kind="ExternalInput")
    w3t_d = nc.dram_tensor("w3t", [NB3, 128, NKC * 128], F32, kind="ExternalInput")
    w1s_d = nc.dram_tensor("w1s", [128, NKC], F32, kind="ExternalInput")
    w2s_d = nc.dram_tensor("w2s", [128, NKC], F32, kind="ExternalInput")
    h1s_d = nc.dram_tensor("h1s", [128, NKC], F32, kind="ExternalInput")
    h2s_d = nc.dram_tensor("h2s", [128, NKC], F32, kind="ExternalInput")
    w3s_d = nc.dram_tensor("w3s", [128, NIC], F32, kind="ExternalInput")
    qbc_d = nc.dram_tensor("qbc", [2, 128], F32, kind="ExternalInput")
    bones_d = nc.dram_tensor("bones", [128, 2], BF16, kind="ExternalInput")
    onescol_d = nc.dram_tensor("onescol", [128, 1], BF16, kind="ExternalInput")
    onesrow_d = nc.dram_tensor("onesrow", [1, 128], F32, kind="ExternalInput")
    miscc_d = nc.dram_tensor("miscc", [1, 3], F32, kind="ExternalInput")
    onesrowb_d = nc.dram_tensor("onesrowb", [1, 128], BF16, kind="ExternalInput")
    rows_d = nc.dram_tensor("rowsv", [2, NTOK], F32)
    w1b_d = nc.dram_tensor("w1b", [NB12, 128, NIC * 256], BF16)
    w2b_d = nc.dram_tensor("w2b", [NB12, 128, NIC * 256], BF16)
    h1b_d = nc.dram_tensor("h1b", [NBH, 128, NKC * 128], BF16)
    h2b_d = nc.dram_tensor("h2b", [NBH, 128, NKC * 128], BF16)
    w3b_d = nc.dram_tensor("w3b", [NB3, 128, NKC * 128], BF16)
    outT_d = nc.dram_tensor("outT", [D, NTOK], F32, kind="ExternalOutput")

    from contextlib import ExitStack
    with ExitStack() as st:
        en = st.enter_context
        wq_sb = en(nc.sbuf_tensor("s_wq", [128, NIC * HID], BF16))
        m_sb = en(nc.sbuf_tensor("s_m", [128, NMC_Q * D], BF16))
        x2b = en(nc.sbuf_tensor("s_x2b", [128, NIC * SGTOK], BF16))
        h0a = en(nc.sbuf_tensor("s_h0a", [128, NKC * SGTOK], BF16))
        g1a = en(nc.sbuf_tensor("s_g1a", [128, NKC * SGTOK], BF16))
        wringA = en(nc.sbuf_tensor("s_wrA", [128, NKC * 128], BF16))
        wringB = en(nc.sbuf_tensor("s_wrB", [128, NKC * 128], BF16))
        xb0 = en(nc.sbuf_tensor("s_xb0", [128, NIC * TILE], BF16))
        xb1 = en(nc.sbuf_tensor("s_xb1", [128, NIC * TILE], BF16))
        silu_b = [en(nc.sbuf_tensor(f"s_silu{i}", [128, TILE], BF16)) for i in range(2)]
        scr0 = en(nc.sbuf_tensor("s_scr0", [128, TILE], F32))
        scr1 = en(nc.sbuf_tensor("s_scr1", [128, TILE], F32))
        r0 = en(nc.sbuf_tensor("s_r0", [1, TILE], F32))
        r1 = en(nc.sbuf_tensor("s_r1", [1, TILE], F32))
        r2 = en(nc.sbuf_tensor("s_r2", [1, TILE], F32))
        rzc = en(nc.sbuf_tensor("s_rzc", [128, 4], F32))
        bocol = en(nc.sbuf_tensor("s_bocol", [128, NIC], F32))
        w1s = en(nc.sbuf_tensor("s_w1s", [128, NKC], F32))
        w2s = en(nc.sbuf_tensor("s_w2s", [128, NKC], F32))
        h1s = en(nc.sbuf_tensor("s_h1s", [128, NKC], F32))
        h2s = en(nc.sbuf_tensor("s_h2s", [128, NKC], F32))
        w3s = en(nc.sbuf_tensor("s_w3s", [128, NIC], F32))
        qbc = en(nc.sbuf_tensor("s_qbc", [2, 128], F32))
        bones = en(nc.sbuf_tensor("s_bones", [128, 2], BF16))
        onescol = en(nc.sbuf_tensor("s_onescol", [128, 1], BF16))
        onesrow = en(nc.sbuf_tensor("s_onesrow", [1, 128], F32))
        miscc = en(nc.sbuf_tensor("s_miscc", [1, 3], F32))
        onesrow_b = en(nc.sbuf_tensor("s_onesrowb", [1, 128], BF16))
        nc.const_aps.register(F32, EPS, miscc[0:1, 0:1])
        nc.const_aps.register(F32, -0.5, miscc[0:1, 1:2])
        nc.const_aps.register(F32, -1.0, miscc[0:1, 2:3])

        ln1_v = h0a[:, 0:NIC * TILE]
        a_v = h0a[:, NIC * TILE:2 * NIC * TILE]
        sq_v = h0a[:, 2 * NIC * TILE:3 * NIC * TILE]
        expq_v = h0a[:, 3 * NIC * TILE:3 * NIC * TILE + NMC_Q * TILE]
        softq_v = h0a[:, 3 * NIC * TILE + NMC_Q * TILE:
                      3 * NIC * TILE + 2 * NMC_Q * TILE]
        off = 3 * NIC * TILE + 2 * NMC_Q * TILE
        expk_v = h0a[:, off:off + TILE]
        vb_v = h0a[:, off + TILE:off + 2 * TILE]
        ssb_v = h0a[:, off + 2 * TILE:off + 2 * TILE + 4 * 128]
        ssb_accf = h0a[:, 18432:19456].bitcast(F32)
        zk_accf = h0a[:, 19456:20480].bitcast(F32)
        lnp_v = [ln1_v, h0a[:, 20480:20480 + NIC * TILE]]
        sqp_v = [sq_v, h0a[:, 24576:24576 + NIC * TILE]]
        a2_v = ln1_v
        rsbc_b = h0a[:, 28672:29184]
        mrsbc_b = h0a[:, 29184:29696]
        r1b = h0a[0:1, 29696:30208]
        r2b = h0a[0:1, 30208:30720]
        scrb = h0a[:, 30720:31232]
        scrb2 = h0a[:, 31232:31744]
        rzq = h0a[:, 31744:32768].bitcast(F32)[0:2, :]
        ln3_v = g1a[:, 0:NIC * SGTOK]
        wk_v = g1a[:, NIC * SGTOK:NIC * SGTOK + NIC * HID]
        wv_v = g1a[:, NIC * SGTOK + NIC * HID:NIC * SGTOK + 2 * NIC * HID]
        wo_v = g1a[:, NIC * SGTOK + 2 * NIC * HID:
                   NIC * SGTOK + 2 * NIC * HID + NMC_Q * D]
        fo = NIC * SGTOK + 2 * NIC * HID + NMC_Q * D
        f_v = g1a[:, fo:fo + NIC * SGTOK]
        sq4_v = g1a[:, fo + NIC * SGTOK:fo + NIC * SGTOK + NIC * TILE]
        out_sbs = [g1a[:, 0:2 * NIC * TILE].bitcast(F32),
                   g1a[:, fo:fo + 2 * NIC * TILE].bitcast(F32)]
        rowsA_pre = h0a[:, 24576:26624].bitcast(F32)
        bcsets = [
            (rsbc_b, mrsbc_b, r1b, r2b),
            (h0a[:, 12288:12800], h0a[:, 12800:13312],
             h0a[0:1, 13312:13824], h0a[0:1, 13824:14336]),
        ]
        rowsB_pre = g1a[:, fo + NIC * SGTOK:
                        fo + NIC * SGTOK + 2048].bitcast(F32)

        pb = [en(nc.psum_tensor(f"pb{i}", [128, 512], F32)) for i in range(8)]
        sems = {s: en(nc.semaphore(s)) for s in ("tq", "tg", "tt", "tv", "ts", "to")}
        B = Build(sems)
        bank_w = [None] * 8
        bank_r = [[] for _ in range(8)]

        def mm_group(bank, mms):
            n = len(mms)
            ev = None
            for i, (o, l, r, w) in enumerate(mms):
                ws = list(w)
                if i == 0:
                    ws += bank_r[bank]
                ev = B.op("tensor",
                          lambda e, o=o, l=l, r=r, s=(i == 0), t=(i == n - 1):
                          e.matmul(o, l, r, start=s, stop=t),
                          waits=ws, inc="tt" if i == n - 1 else None)
            bank_w[bank] = ev
            bank_r[bank] = []
            return ev

        def bank_read(eng, bank, f, waits=()):
            inc = {"vector": "tv", "scalar": "ts"}[eng]
            ev = B.op(eng, f, waits=list(waits) + [bank_w[bank]], inc=inc)
            bank_r[bank].append(ev)
            return ev

        def cdma(out_ap, in_ap, waits=()):
            return B.op("gpsimd", lambda e, o=out_ap, i=in_ap: e.dma_start(o, i),
                        waits=list(waits) + [("tg", B.cnt["tg"])],
                        inc="tg", inck=16)

        def qdma(out_ap, in_ap, waits=(), inc="tq"):
            return B.op("sync", lambda e, o=out_ap, i=in_ap: e.dma_start(o, i),
                        waits=list(waits) + [(inc, B.cnt[inc])], inc=inc, inck=16)

        def part3(dram_ap):
            return dram_ap.rearrange("(a p) c -> p a c", p=128)

        def sb3(sb_ap, c):
            return sb_ap.rearrange("p (a c) -> p a c", c=c)

        ev_wq = cdma(sb3(wq_sb[:, :], HID), part3(wq_d.ap()))
        ev_wk = cdma(sb3(wk_v, HID), part3(wk_d.ap()))
        ev_wv = cdma(sb3(wv_v, HID), part3(wv_d.ap()))
        ev_wo = cdma(sb3(wo_v, D), part3(wo_d.ap()))
        ev_consts = None
        for sb, d in ((bocol, bocol_d), (w1s, w1s_d), (w2s, w2s_d), (h1s, h1s_d),
                      (h2s, h2s_d), (w3s, w3s_d), (qbc, qbc_d), (bones, bones_d),
                      (onescol, onescol_d), (onesrow, onesrow_d),
                      (miscc, miscc_d), (onesrow_b, onesrowb_d)):
            ev_consts = qdma(sb[:, :], d.ap())

        xbuf = [xb0, xb1]
        xb_ready = [None, None]
        xb_users = [[], []]

        def load_xb(t):
            p = t % 2
            src = part3(xT_d.ap()[:, t * TILE:(t + 1) * TILE])
            xb_ready[p] = cdma(sb3(xbuf[p][:, :], TILE), src,
                               waits=list(xb_users[p]))
            xb_users[p] = []

        rows_war = []
        rowsb_war = [[], []]
        bc_idx = [0]

        def ln_stats(src_fn, sq_fn, src_ev, bsum, bsq, sq_eng="scalar",
                     src3=None, sq3=None):
            nonlocal rows_war
            sq_w = [src_ev(ic) for ic in range(NIC)]
            if sq_eng == "scalar":
                ev_sq = B.op("scalar", lambda e: e.square(sq3, src3),
                             waits=sq_w, inc="ts")
            else:
                ev_sq = B.op("vector", lambda e: e.tensor_mul(sq3, src3, src3),
                             waits=sq_w, inc="tv")
            mm_group(bsum, [(pb[bsum][0:1, :], onescol[:, :], src_fn(ic),
                             [src_ev(ic), ev_consts]) for ic in range(NIC)])
            mm_group(bsq, [(pb[bsq][0:1, :], onescol[:, :], sq_fn(ic),
                            [ev_sq]) for ic in range(NIC)])
            war = list(rows_war)
            rows_war = []
            bank_read("vector", bsum,
                      lambda e, b=bsum: e.tensor_scalar_mul(r0[:, :], pb[b][0:1, :],
                                                            INV_D),
                      waits=war)
            bank_read("vector", bsq,
                      lambda e, b=bsq: e.tensor_scalar_mul(r1[:, :], pb[b][0:1, :],
                                                           INV_D))
            B.op("vector", lambda e: e.tensor_mul(r2[:, :], r0[:, :], r0[:, :]))
            ev_var = B.op("vector",
                          lambda e: e.tensor_sub(r1[:, :], r1[:, :], r2[:, :]),
                          inc="tv")
            ev_ln = B.op("scalar",
                         lambda e: e.activation(r2[:, :], r1[:, :], AF.Ln, bias=EPS),
                         waits=[ev_var, ev_consts], inc="ts")
            ev_rs = B.op("scalar",
                         lambda e: e.activation(r1[:, :], r2[:, :], AF.Exp, scale=-0.5),
                         waits=[ev_ln], inc="ts")
            return B.op("vector", lambda e: e.tensor_mul(r2[:, :], r0[:, :], r1[:, :]),
                        waits=[ev_rs], inc="tv")

        def ln_bc32(ev_rows, b_rs, b_mrs):
            e1 = mm_group(b_rs, [(pb[b_rs][:, :], onesrow[:, :], r1[:, :],
                                  [ev_rows, ev_consts])])
            e2 = mm_group(b_mrs, [(pb[b_mrs][:, :], onesrow[:, :], r2[:, :], [])])
            rows_war.extend([e1, e2])

        def ln_bc(ev_rows, b_rs, b_mrs, rowsrc=None):
            si = bc_idx[0] = 1 - bc_idx[0]
            rsb, mrsb, c1b, c2b = bcsets[si]
            ra, rb = rowsrc if rowsrc is not None else (r1[:, :], r2[:, :])
            wb = list(rowsb_war[si])
            rowsb_war[si] = []
            ev_c1 = B.op("vector", lambda e, ra=ra, d=c1b: e.tensor_copy(d, ra),
                         waits=[ev_rows] + wb, inc="tv")
            ev_c2 = B.op("vector", lambda e, rb=rb, d=c2b: e.tensor_copy(d, rb),
                         inc="tv")
            rows_war.append(ev_c2)
            e1 = mm_group(b_rs, [(pb[b_rs][:, :], onesrow_b[:, :], c1b,
                                  [ev_c1, ev_consts])])
            e2 = mm_group(b_mrs, [(pb[b_mrs][:, :], onesrow_b[:, :], c2b,
                                   [ev_c2])])
            rowsb_war[si].extend([e1, e2])
            bank_read("vector", b_rs,
                      lambda e, b=b_rs, d=rsb: e.tensor_copy(d, pb[b][:, :]))
            ev = bank_read("vector", b_mrs,
                           lambda e, b=b_mrs, d=mrsb: e.tensor_copy(d, pb[b][:, :]))
            return (ev, si)

        def ln_apply32(src_fn, dst_fn, b_rs, b_mrs, src_ev, extra_fn=None,
                       first_wait=None):
            ev = None
            for ic in range(NIC):
                s_ap, d_ap = src_fn(ic), dst_fn(ic)
                w = [src_ev(ic)]
                if ic == 0 and first_wait is not None:
                    w.append(first_wait)
                bank_read("vector", b_rs,
                          lambda e, s=s_ap, b=b_rs: e.tensor_mul(scr0[:, :], s,
                                                                 pb[b][:, :]),
                          waits=w)
                if extra_fn is None:
                    ev = bank_read("vector", b_mrs,
                                   lambda e, d=d_ap, b=b_mrs: e.tensor_sub(
                                       d, scr0[:, :], pb[b][:, :]))
                else:
                    x_ap = extra_fn(ic)
                    bank_read("vector", b_mrs,
                              lambda e, b=b_mrs: e.tensor_sub(scr1[:, :], scr0[:, :],
                                                              pb[b][:, :]))
                    ev = B.op("vector",
                              lambda e, d=d_ap, x=x_ap: e.tensor_add(d, scr1[:, :], x),
                              inc="tv")
            return ev

        def ln_apply(src3, dst3, ev_bc, ev_srcs, extra3=None, first_wait=None,
                     tmp3=None):
            ev_bc, si = ev_bc
            nchunk = src3.shape[1]
            rs3 = bcsets[si][0].unsqueeze(1).broadcast_to((128, nchunk, TILE))
            mrs3 = bcsets[si][1].unsqueeze(1).broadcast_to((128, nchunk, TILE))
            w = list(ev_srcs) + [ev_bc]
            if first_wait is not None:
                w.append(first_wait)
            t3 = dst3 if tmp3 is None else tmp3
            B.op("vector", lambda e: e.tensor_mul(t3, src3, rs3), waits=w)
            if extra3 is None:
                return B.op("vector", lambda e: e.tensor_sub(dst3, t3, mrs3),
                            inc="tv")
            B.op("vector", lambda e: e.tensor_sub(t3, t3, mrs3))
            return B.op("vector", lambda e: e.tensor_add(dst3, t3, extra3),
                        inc="tv")

        def c3(ap, n=TILE):
            return ap.rearrange("p (c n) -> p c n", n=n)

        def sg3(ap, nt):
            return ap.rearrange("p (c g) -> p c g", g=SGTOK)[:, :, nt * TILE:(nt + 1) * TILE]


        slots = {"A": {"ap": wringA, "users": []},
                 "B": {"ap": wringB, "users": []}}

        bmap = {id(w1t_d): w1b_d, id(w2t_d): w2b_d, id(h1t_d): h1b_d,
                id(h2t_d): h2b_d, id(w3t_d): w3b_d}
        pp_state = {"done": {"tq": None, "to": None}, "n": 0}
        pp_written = set()

        def wload(sl, dram, blk, ncols, half=None, sg=1):
            s = slots[sl]
            dst = s["ap"][:, 0:ncols] if half is None else \
                s["ap"][:, half * ncols:(half + 1) * ncols]
            if sg < 2:
                src_ap = dram.ap()[blk, :, 0:ncols]
            else:
                src_ap = bmap[id(dram)].ap()[blk, :, 0:ncols]
            w = list(s["users"])
            if sg >= 2:
                w += [pp_state["done"]["tq"], pp_state["done"]["to"]]
            ev = cdma(dst, src_ap, waits=w)
            s["users"] = []
            if sg == 0 and (id(dram), blk) not in pp_written:
                pp_written.add((id(dram), blk))
                q = ("tq", "to")[pp_state["n"] % 2]
                pp_state["n"] += 1
                ev_s = qdma(bmap[id(dram)].ap()[blk, :, 0:ncols], dst,
                            waits=[ev], inc=q)
                s["users"].append(ev_s)
                pp_state["done"][q] = ev_s
            return ev


        ev_init = B.op("vector", lambda e: e.memset(ssb_accf[:, :], 0.0), inc="tv")
        ev_init2 = B.op("vector", lambda e: e.memset(zk_accf[0:1, :], 0.0), inc="tv")

        def p1_chain_stats(t):
            p4 = 4 * (t % 2)
            xb = xbuf[t % 2]
            xev = xb_ready[t % 2]
            return ln_stats(lambda i: xb[:, i * TILE:(i + 1) * TILE],
                            lambda i: sqp_v[t % 2][:, i * TILE:(i + 1) * TILE],
                            lambda i, ev=xev: ev, p4, p4 + 1,
                            src3=c3(xb[:, :]), sq3=c3(sqp_v[t % 2]))

        def p1_chain_bc(t, ev_rows):
            p4 = 4 * (t % 2)
            sl = slice(t * TILE, (t + 1) * TILE)
            e1 = qdma(rows_d.ap()[0:1, sl], r1[:, :], waits=[ev_rows])
            e2 = qdma(rows_d.ap()[1:2, sl], r2[:, :])
            rows_war.append(e2)
            return ln_bc(ev_rows, p4 + 2, p4 + 3)

        def p1_apply_mul(t, ev_bc):
            ev_bc, si = ev_bc
            xb = xbuf[t % 2]
            rs3 = bcsets[si][0].unsqueeze(1).broadcast_to((128, NIC, TILE))
            B.op("vector",
                 lambda e: e.tensor_mul(c3(lnp_v[t % 2]), c3(xb[:, :]), rs3),
                 waits=[xb_ready[t % 2], ev_bc])
            return si

        def p1_apply_sub(t, si):
            mrs3 = bcsets[si][1].unsqueeze(1).broadcast_to((128, NIC, TILE))
            ev = B.op("vector",
                      lambda e: e.tensor_sub(c3(lnp_v[t % 2]), c3(lnp_v[t % 2]),
                                             mrs3),
                      inc="tv")
            xb_users[t % 2].append(ev)
            return ev

        def p1_groups(t, g, ev_ln1):
            p4 = 4 * (t % 2)
            lnv = lnp_v[t % 2]
            bk, bv, bs, bz = p4, p4 + 1, p4 + 2, p4 + 3
            mm_group(bk, [
                (pb[bk][:, :],
                 lnv[:, ic * TILE + g * GRP: ic * TILE + (g + 1) * GRP],
                 wk_v[:, ic * HID:(ic + 1) * HID],
                 [ev_ln1, ev_wk] if ic == 0 else ()) for ic in range(NIC)])
            mm_group(bv, [
                (pb[bv][:, :],
                 lnv[:, ic * TILE + g * GRP: ic * TILE + (g + 1) * GRP],
                 wv_v[:, ic * HID:(ic + 1) * HID],
                 [ev_ln1, ev_wv] if ic == 0 else ()) for ic in range(NIC)])
            ev_ek = bank_read("scalar", bk,
                              lambda e, b=bk: e.activation(expk_v, pb[b][:, :],
                                                           AF.Exp))
            ev_vb = bank_read("scalar", bv,
                              lambda e, b=bv: e.copy(vb_v, pb[b][:, :]))
            for j in range(4):
                mm_group(bs, [(pb[bs][:, j * 128:(j + 1) * 128],
                               vb_v[:, j * 128:(j + 1) * 128],
                               expk_v[:, j * 128:(j + 1) * 128],
                               [ev_vb, ev_ek])])
            bank_read("vector", bs,
                      lambda e, b=bs: e.tensor_add(ssb_accf[:, :], ssb_accf[:, :],
                                                   pb[b][:, :]),
                      waits=[ev_init])
            mm_group(bz, [(pb[bz][0:1, :], onescol[:, :], expk_v[:, :],
                           [ev_ek, ev_consts])])
            bank_read("vector", bz,
                      lambda e, b=bz: e.tensor_add(zk_accf[0:1, :], zk_accf[0:1, :],
                                                   pb[b][0:1, :]),
                      waits=[ev_init2])

        load_xb(0)
        if NTILES > 1:
            load_xb(1)
        ev_rows_n = p1_chain_stats(0)
        ev_bc_n = p1_chain_bc(0, ev_rows_n)
        si_n = p1_apply_mul(0, ev_bc_n)
        ev_ln1_cur = p1_apply_sub(0, si_n)
        for t in range(NTILES):
            nxt = t + 1 < NTILES
            p1_groups(t, 0, ev_ln1_cur)
            if nxt:
                ev_rows_n = p1_chain_stats(t + 1)
            p1_groups(t, 1, ev_ln1_cur)
            if t + 2 < NTILES:
                load_xb(t + 2)
            if nxt:
                ev_bc_n = p1_chain_bc(t + 1, ev_rows_n)
            p1_groups(t, 2, ev_ln1_cur)
            if nxt:
                si_n = p1_apply_mul(t + 1, ev_bc_n)
            p1_groups(t, 3, ev_ln1_cur)
            if nxt:
                ev_ln1_cur = p1_apply_sub(t + 1, si_n)

        ev_ssb = []
        for j in range(4):
            ev_ssb.append(B.op("vector",
                               lambda e, j=j: e.tensor_copy(
                                   ssb_v[:, j * 128:(j + 1) * 128],
                                   ssb_accf[:, j * 128:(j + 1) * 128]),
                               inc="tv"))
        for ev in rows_war:
            B.wait("vector", ev)
        rows_war = []
        ev_rzk = B.op("vector", lambda e: e.reciprocal(r1[:, :], zk_accf[0:1, :]),
                      inc="tv")
        ev_rz = None
        for j in range(4):
            ev_rz = B.op("tensor",
                         lambda e, j=j: e.matmul(pb[5][:, j:j + 1],
                                                 r1[0:1, j * 128:(j + 1) * 128],
                                                 onesrow[0:1, 0:1],
                                                 start=True, stop=True),
                         waits=[ev_rzk, ev_consts] + (bank_r[5] if j == 0 else []),
                         inc="tt" if j == 3 else None)
        bank_w[5] = ev_rz
        bank_r[5] = []
        ev_rzc = bank_read("vector", 5,
                           lambda e: e.tensor_scalar_mul(rzc[:, :], pb[5][:, 0:4],
                                                         SCALE))
        rows_war.append(ev_rz)
        ev_m_done = None
        for j in range(4):
            for half in range(2):
                mm_group(6, [(pb[6][0:64, :], ssb_v[0:64, j * 128:j * 128 + 64],
                              wo_v[0:64, j * D + half * 512:j * D + (half + 1) * 512],
                              [ev_ssb[j], ev_wo])])
                mm_group(7, [(pb[7][64:128, :],
                              ssb_v[64:128, j * 128 + 64:j * 128 + 128],
                              wo_v[64:128, j * D + half * 512:j * D + (half + 1) * 512],
                              [ev_ssb[j], ev_wo])])
                bank_read("vector", 6,
                          lambda e, j=j, h=half: e.tensor_scalar_mul(
                              m_sb[0:64, j * D + h * 512:j * D + (h + 1) * 512],
                              pb[6][0:64, :], rzc[0:64, j:j + 1]),
                          waits=[ev_rzc])
                ev_m_done = bank_read("vector", 7,
                                      lambda e, j=j, h=half: e.tensor_scalar_mul(
                                          m_sb[64:128, j * D + h * 512:
                                               j * D + (h + 1) * 512],
                                          pb[7][64:128, :], rzc[64:128, j:j + 1]),
                                      waits=[ev_rzc])

        ev_od = [None, None]
        rz_war = None
        silu_war = [None, None]
        rows_pre_users = [[], []]

        rows_pre_ev = [None, None]

        def preload_rows(t):
            par = t % 2
            buf = rowsA_pre if par == 0 else rowsB_pre
            sl = slice(t * TILE, (t + 1) * TILE)
            qdma(buf[0:1, 0:TILE], rows_d.ap()[0:1, sl],
                 waits=list(rows_pre_users[par]))
            rows_pre_ev[par] = qdma(buf[0:1, TILE:2 * TILE], rows_d.ap()[1:2, sl])
            rows_pre_users[par] = []

        def a_ln1(t):
            par = t % 2
            buf = rowsA_pre if par == 0 else rowsB_pre
            xev = xb_ready[par]
            xb = xbuf[par]
            ev_bc = ln_bc(rows_pre_ev[par], 6, 7,
                          rowsrc=(buf[0:1, 0:TILE], buf[0:1, TILE:2 * TILE]))
            rows_pre_users[par].append(ev_bc[0])
            return ln_apply(c3(xb[:, :]), c3(lnp_v[par]), ev_bc, [xev])

        def a_q(t, ev_ln1):
            nonlocal rz_war
            lnv = lnp_v[t % 2]
            ev_expq = [None] * NMC_Q
            ev_softq = [None] * NMC_Q
            for mc in range(NMC_Q):
                mm_group(mc, [(pb[mc][:, :],
                               wq_sb[:, ic * HID + mc * 128:ic * HID + (mc + 1) * 128],
                               lnv[:, ic * TILE:(ic + 1) * TILE],
                               [ev_ln1, ev_wq] if ic == 0 else ())
                              for ic in range(NIC)])
                ev_expq[mc] = bank_read(
                    "scalar", mc,
                    lambda e, m=mc: e.activation(
                        expq_v[:, m * TILE:(m + 1) * TILE], pb[m][:, :], AF.Exp))
            for mc in range(NMC_Q):
                mm_group(mc, [(pb[mc][0:2, :], bones[:, :],
                               expq_v[:, mc * TILE:(mc + 1) * TILE],
                               [ev_expq[mc], ev_consts])])
            for mc in range(NMC_Q):
                ev_lnz = bank_read("scalar", mc,
                                   lambda e, m=mc: e.activation(rzq[:, :],
                                                                pb[m][0:2, :], AF.Ln),
                                   waits=[rz_war] if rz_war else ())
                rz_war = mm_group(mc, [(pb[mc][:, :], qbc[:, :], rzq[:, :],
                                        [ev_lnz, ev_consts])])
                dstb = 4 + (mc % 2)
                ev_bcz = bank_w[mc]
                ev_ez = B.op("scalar",
                             lambda e, m=mc, d=dstb: e.activation(
                                 pb[d][:, :], pb[m][:, :], AF.Exp, scale=-1.0),
                             waits=[ev_bcz] + bank_r[dstb] +
                             ([bank_w[dstb]] if bank_w[dstb] else []),
                             inc="ts")
                bank_r[mc].append(ev_ez)
                bank_w[dstb] = ev_ez
                bank_r[dstb] = []
                ev_softq[mc] = bank_read(
                    "vector", dstb,
                    lambda e, m=mc, d=dstb: e.tensor_mul(
                        softq_v[:, m * TILE:(m + 1) * TILE],
                        expq_v[:, m * TILE:(m + 1) * TILE], pb[d][:, :]))
            return ev_softq

        def a_attn(ev_softq, a_dst, ocs=None, ev_a=None):
            if ev_a is None:
                ev_a = [None] * NIC
            for oc in (ocs if ocs is not None else range(NIC)):
                ab = 4 + (oc % 2)
                mm_group(ab, [(pb[ab][:, :],
                               m_sb[:, mc * D + oc * 128:mc * D + (oc + 1) * 128],
                               softq_v[:, mc * TILE:(mc + 1) * TILE],
                               [ev_softq[mc]] + ([ev_m_done] if oc == 0 else []))
                              for mc in range(NMC_Q)])
                if oc % 2 == 0:
                    ev_a[oc] = bank_read(
                        "vector", ab,
                        lambda e, o=oc, b=ab, d=a_dst: e.tensor_scalar_add(
                            d[:, o * TILE:(o + 1) * TILE], pb[b][:, :],
                            bocol[:, o:o + 1]),
                        waits=[ev_consts])
                else:
                    ev_a[oc] = bank_read(
                        "scalar", ab,
                        lambda e, o=oc, b=ab, d=a_dst: e.add(
                            d[:, o * TILE:(o + 1) * TILE], pb[b][:, :],
                            bocol[:, o:o + 1]),
                        waits=[ev_consts])
            return ev_a

        class LN23:

            def __init__(self, t, nt, a_src, ev_a):
                self.t, self.nt, self.a_src, self.ev_a = t, nt, a_src, ev_a

            def p1(self):
                t, nt = self.t, self.nt
                self.ev_rows2 = ln_stats(
                    lambda i: self.a_src[:, i * TILE:(i + 1) * TILE],
                    lambda i: sqp_v[t % 2][:, i * TILE:(i + 1) * TILE],
                    lambda i: self.ev_a[i], 6, 7,
                    src3=c3(self.a_src), sq3=c3(sqp_v[t % 2]))

            def p2(self):
                t, nt = self.t, self.nt
                xb = xbuf[t % 2]
                ev_bc2 = ln_bc(self.ev_rows2, 6, 7)
                self.ev_x2 = ln_apply(c3(self.a_src), sg3(x2b[:, :], nt), ev_bc2,
                                      self.ev_a, extra3=c3(xb[:, :]))
                xb_users[t % 2].append(self.ev_x2)

            def p3(self):
                t, nt = self.t, self.nt
                self.ev_rows3 = ln_stats(
                    lambda i: x2b[:, i * SGTOK + nt * TILE:
                                  i * SGTOK + (nt + 1) * TILE],
                    lambda i: sqp_v[t % 2][:, i * TILE:(i + 1) * TILE],
                    lambda i, ev=self.ev_x2: ev, 6, 7,
                    src3=sg3(x2b[:, :], nt), sq3=c3(sqp_v[t % 2]))

            def p4(self):
                nt = self.nt
                ev_bc3 = ln_bc(self.ev_rows3, 6, 7)
                self.ev_ln3 = ln_apply(sg3(x2b[:, :], nt), sg3(ln3_v, nt),
                                       ev_bc3, [self.ev_x2],
                                       first_wait=ev_od[0])
                return self.ev_ln3

        preload_rows(0)
        if NTILES > 1:
            preload_rows(1)
        a2B_v = lnp_v[1]
        for sg in range(NSG):
            tA, tB = sg * 2, sg * 2 + 1
            load_xb(tA)
            load_xb(tB)
            evA_ln1 = a_ln1(tA)
            sqA = a_q(tA, evA_ln1)
            evB_ln1 = a_ln1(tB)
            evA_a = a_attn(sqA, a_v)
            sqB = a_q(tB, evB_ln1)
            chA = LN23(tA, 0, a_v, evA_a)
            chA.p1()
            evB_a = a_attn(sqB, a2B_v, ocs=range(0, 4))
            chA.p2()
            a_attn(sqB, a2B_v, ocs=range(4, NIC), ev_a=evB_a)
            chA.p3()
            ev_ln3_nt = [chA.p4(), None]
            chB = LN23(tB, 1, a2B_v, evB_a)
            w12_hooks = [chB.p1, chB.p2, chB.p3,
                         lambda: ev_ln3_nt.__setitem__(1, chB.p4())]

            ev_h0 = None
            ev_w1r = [None, None]
            ev_w2r = [None, None]
            ev_w1r[0] = wload("A", w1t_d, 0, NIC * 256, half=0, sg=sg)
            ev_w2r[0] = wload("B", w2t_d, 0, NIC * 256, half=0, sg=sg)
            if NB12 > 1:
                ev_w1r[1] = wload("A", w1t_d, 1, NIC * 256, half=1, sg=sg)
                ev_w2r[1] = wload("B", w2t_d, 1, NIC * 256, half=1, sg=sg)
            sched = [(0, (0,)), (1, (0,))] + \
                    [(bb, (0, 1)) for bb in range(2, NB12)] + \
                    [(0, (1,)), (1, (1,))]
            it = 0
            for pos, (b, nts) in enumerate(sched):
                hb = pos % 2
                w1ap = wringA[:, hb * NIC * 256:(hb + 1) * NIC * 256]
                w2ap = wringB[:, hb * NIC * 256:(hb + 1) * NIC * 256]
                ev_last_use = None
                for nt in nts:
                    for j in range(2):
                        mc = 2 * b + j
                        b1, b2 = [(0, 1), (2, 3), (4, 5)][it % 3]
                        mm_group(b1, [(pb[b1][:, :],
                                       w1ap[:, ic * 256 + j * 128:
                                            ic * 256 + (j + 1) * 128],
                                       ln3_v[:, ic * SGTOK + nt * TILE:
                                             ic * SGTOK + (nt + 1) * TILE],
                                       [ev_w1r[hb], ev_ln3_nt[nt]]
                                       if ic == 0 else ())
                                      for ic in range(NIC)])
                        ev_last_use = mm_group(
                            b2, [(pb[b2][:, :],
                                  w2ap[:, ic * 256 + j * 128:
                                       ic * 256 + (j + 1) * 128],
                                  ln3_v[:, ic * SGTOK + nt * TILE:
                                        ic * SGTOK + (nt + 1) * TILE],
                                  [ev_w2r[hb]] if ic == 0 else ())
                                 for ic in range(NIC)])
                        sp = silu_b[it % 2]
                        ev_si = bank_read(
                            "scalar", b1,
                            lambda e, m=mc, s=sp, b_=b1: e.activation(
                                s[:, :], pb[b_][:, :], AF.Silu,
                                scale=w1s[:, m:m + 1]),
                            waits=[ev_consts, silu_war[it % 2]]
                            if silu_war[it % 2] else [ev_consts])
                        ev_h0 = bank_read(
                            "vector", b2,
                            lambda e, m=mc, n=nt, s=sp, b_=b2:
                            e.scalar_tensor_tensor(
                                h0a[:, m * SGTOK + n * TILE:
                                    m * SGTOK + (n + 1) * TILE],
                                pb[b_][:, :], w2s[:, m:m + 1], s[:, :],
                                ALU.mult, ALU.mult),
                            waits=[ev_si])
                        silu_war[it % 2] = ev_h0
                        it += 1
                        if w12_hooks:
                            w12_hooks.pop(0)()
                slots["A"]["users"].append(ev_last_use)
                slots["B"]["users"].append(ev_last_use)
                if pos + 2 < len(sched):
                    nb = sched[pos + 2][0]
                    ev_w1r[hb] = wload("A", w1t_d, nb, NIC * 256, half=hb, sg=sg)
                    ev_w2r[hb] = wload("B", w2t_d, nb, NIC * 256, half=hb, sg=sg)

            def h_stage(dram, s_col, dst_base, rhs_base, ev_rhs_done):
                ev_done = None
                ev_r = [None, None]
                ev_r[0] = wload("A", dram, 0, NKC * 128, sg=sg)
                if NBH > 1:
                    ev_r[1] = wload("B", dram, 1, NKC * 128, sg=sg)
                for b in range(NBH):
                    sl = "A" if b % 2 == 0 else "B"
                    wap = slots[sl]["ap"]
                    ev_use = None
                    for nt in range(2):
                        bk = 4 + (b * 2 + nt) % 4
                        ev_use = mm_group(bk, [
                            (pb[bk][:, :], wap[:, kc * 128:(kc + 1) * 128],
                             rhs_base[:, kc * SGTOK + nt * TILE:
                                      kc * SGTOK + (nt + 1) * TILE],
                             [ev_r[b % 2], ev_rhs_done] if kc == 0 else ())
                            for kc in range(NKC)])
                        ev_done = bank_read(
                            "scalar", bk,
                            lambda e, m=b, n=nt, b_=bk, dst=dst_base, sc=s_col:
                            e.activation(
                                dst[:, m * SGTOK + n * TILE:m * SGTOK + (n + 1) * TILE],
                                pb[b_][:, :], AF.Gelu, scale=sc[:, m:m + 1]),
                            waits=[ev_consts])
                    slots[sl]["users"].append(ev_use)
                    if b + 2 < NBH:
                        ev_r[b % 2] = wload(sl, dram, b + 2, NKC * 128, sg=sg)
                return ev_done

            ev_g1 = h_stage(h1t_d, h1s, g1a, h0a, ev_h0)
            ev_g2 = h_stage(h2t_d, h2s, h0a, g1a, ev_g1)

            ev_rw3 = [None, None]
            ev_rw3[0] = wload("A", w3t_d, 0, NKC * 128, sg=sg)
            if NB3 > 1:
                ev_rw3[1] = wload("B", w3t_d, 1, NKC * 128, sg=sg)
            ev_f = [[None] * NIC, [None] * NIC]
            for b in range(NB3):
                oc = b
                sl = "A" if b % 2 == 0 else "B"
                wap = slots[sl]["ap"]
                ev_use = None
                for nt in range(2):
                    bk = 6 + (b * 2 + nt) % 2
                    ev_use = mm_group(bk, [
                        (pb[bk][:, :], wap[:, kc * 128:(kc + 1) * 128],
                         h0a[:, kc * SGTOK + nt * TILE:kc * SGTOK + (nt + 1) * TILE],
                         [ev_rw3[b % 2], ev_g2] if kc == 0 else ())
                        for kc in range(NKC)])
                    eng_ = "vector" if oc % 2 == 0 else "scalar"
                    def f_ev(e, o=oc, n=nt, b_=bk):
                        dst = f_v[:, o * SGTOK + n * TILE:
                                  o * SGTOK + (n + 1) * TILE]
                        if isinstance(e, type(None)):
                            return None
                        try:
                            return e.tensor_scalar_mul(dst, pb[b_][:, :],
                                                       w3s[:, o:o + 1])
                        except AttributeError:
                            return e.mul(dst, pb[b_][:, :], w3s[:, o:o + 1])
                    ev_f[nt][oc] = bank_read(
                        eng_, bk, f_ev,
                        waits=[ev_consts, ev_od[1]] if (b == 0 and nt == 0)
                        else [ev_consts])
                slots[sl]["users"].append(ev_use)
                if b + 2 < NB3:
                    ev_rw3[b % 2] = wload(sl, w3t_d, b + 2, NKC * 128, sg=sg)

            def ln4_stats(nt):
                sqr = sq4_v if nt == 0 else sqp_v[0]
                bs = (4, 5) if nt == 0 else (6, 7)
                return ln_stats(
                    lambda i: f_v[:, i * SGTOK + nt * TILE:
                                  i * SGTOK + (nt + 1) * TILE],
                    lambda i, s=sqr: s[:, i * TILE:(i + 1) * TILE],
                    lambda i: ev_f[nt][i], bs[0], bs[1], sq_eng="vector",
                    src3=sg3(f_v, nt), sq3=c3(sqr))

            def ln4_out(nt, ev_bc4):
                t = sg * 2 + nt
                tmp = sq4_v if nt == 0 else sqp_v[0]
                ev_out = ln_apply(
                    sg3(f_v, nt), c3(out_sbs[nt][:, :]), ev_bc4, ev_f[nt],
                    extra3=sg3(x2b[:, :], nt), first_wait=ev_od[nt],
                    tmp3=c3(tmp))
                ev_od[nt] = qdma(
                    part3(outT_d.ap()[:, t * TILE:(t + 1) * TILE]),
                    sb3(out_sbs[nt][:, :], TILE), waits=[ev_out], inc="to")

            rows_pre_users[0].append(chB.ev_rows3)
            rows_pre_users[0].append(ev_f[1][NIC - 1])
            rows_pre_users[1].append(ev_f[1][NIC - 1])
            ev_r40 = ln4_stats(0)
            ev_bc40 = ln_bc(ev_r40, 4, 5)
            ev_r41 = ln4_stats(1)
            if sg + 1 < NSG:
                preload_rows(2 * (sg + 1))
            ev_bc41 = ln_bc(ev_r41, 6, 7)
            ln4_out(0, ev_bc40)
            if sg + 1 < NSG:
                preload_rows(2 * (sg + 1) + 1)
            ln4_out(1, ev_bc41)

        to_total = B.cnt["to"]
        B.op("sync", lambda e, v=to_total: e.wait_ge(sems["to"], v))

        n_inst = sum(len(v) for v in B.prog.values())
        print("sbuf bytes remaining:", nc.sbuf_bytes_remaining,
              "thunks:", {k: len(v) for k, v in B.prog.items()}, "total", n_inst)

        with nc.Block() as block:
            @block.tensor
            def _(eng):
                for f in B.prog["tensor"]:
                    f(eng)

            @block.vector
            def _(eng):
                for f in B.prog["vector"]:
                    f(eng)

            @block.scalar
            def _(eng):
                for f in B.prog["scalar"]:
                    f(eng)

            @block.sync
            def _(eng):
                for f in B.prog["sync"]:
                    f(eng)

            @block.gpsimd
            def _(eng):
                for f in B.prog["gpsimd"]:
                    f(eng)

    return nc


def host_layout(inputs):
    NIC = D // 128
    NKC = DFF // 128
    x = np.asarray(inputs["L0_attn_x"], dtype=np.float32)

    def col(v, n):
        return np.ascontiguousarray(np.asarray(v, np.float32).reshape(n, 128).T)

    def stream(w, nk, bw):
        wT = np.ascontiguousarray(np.asarray(w, np.float32).T)
        nb = wT.shape[1] // bw
        return np.ascontiguousarray(
            wT.reshape(nk, 128, nb, bw).transpose(2, 1, 0, 3).reshape(nb, 128, nk * bw))

    qbc = np.zeros((2, 128), np.float32)
    qbc[0, 0:64] = 1.0
    qbc[1, 64:128] = 1.0
    bones = np.zeros((128, 2), ml_dtypes.bfloat16)
    bones[0:64, 0] = 1.0
    bones[64:128, 1] = 1.0

    shared = {
        "wq": np.ascontiguousarray(np.asarray(inputs["L0_attn_Wq"], np.float32)),
        "wk": np.ascontiguousarray(np.asarray(inputs["L0_attn_Wk"], np.float32)),
        "wv": np.ascontiguousarray(np.asarray(inputs["L0_attn_Wv"], np.float32)),
        "wo": np.ascontiguousarray(np.asarray(inputs["L0_attn_Wo"], np.float32)),
        "bocol": col(inputs["L0_attn_bo"], NIC),
        "w1t": stream(inputs["L0_ffn_w1_w"], NIC, 256),
        "w2t": stream(inputs["L0_ffn_w2_w"], NIC, 256),
        "h1t": stream(inputs["L0_ffn_h1_w"], NKC, 128),
        "h2t": stream(inputs["L0_ffn_h2_w"], NKC, 128),
        "w3t": stream(inputs["L0_ffn_w3_w"], NKC, 128),
        "w1s": col(inputs["L0_ffn_w1_s"], NKC),
        "w2s": col(inputs["L0_ffn_w2_s"], NKC),
        "h1s": col(inputs["L0_ffn_h1_s"], NKC),
        "h2s": col(inputs["L0_ffn_h2_s"], NKC),
        "w3s": col(inputs["L0_ffn_w3_s"], NIC),
        "qbc": qbc,
        "bones": bones,
        "onescol": np.ones((128, 1), ml_dtypes.bfloat16),
        "onesrow": np.ones((1, 128), np.float32),
        "miscc": np.array([[EPS, -0.5, -1.0]], np.float32),
        "onesrowb": np.ones((1, 128), ml_dtypes.bfloat16),
    }
    return [dict(shared, xT=np.ascontiguousarray(x[b].T)) for b in range(x.shape[0])]


def kernel(**inputs):
    in_maps = host_layout(inputs)
    nc = build_nc()
    res = run_bass_kernel_spmd(nc, in_maps, core_ids=list(range(len(in_maps))))
    out = np.empty((len(in_maps), NTOK, D), np.float32)
    for b in range(len(in_maps)):
        out[b] = res.results[b]["outT"].T
    return out


# version 18
# speedup vs baseline: 1.1295x; 1.1295x over previous
import numpy as np
import ml_dtypes
import concourse.bass as bass
import concourse.mybir as mybir
from concourse.bass_utils import run_bass_kernel_spmd

F32 = mybir.dt.float32
BF16 = mybir.dt.bfloat16
AF = mybir.ActivationFunctionType
ALU = mybir.AluOpType

NCORES = 8
NTOK = 8192
D = 1024
HID = 512
DFF = 4096
TILE = 512
GRP = 128
SGTOK = 1024
SCALE = 64.0 ** -0.5
EPS = 1e-5

ENGS = ("tensor", "vector", "scalar", "sync", "gpsimd")


class Build:

    def __init__(self, sems):
        self.sems = sems
        self.cnt = {k: 0 for k in sems}
        self.prog = {e: [] for e in ENGS}
        self.hwm = {e: {} for e in ENGS}

    def wait(self, eng, ev):
        if ev is None:
            return
        sem, val = ev
        if self.hwm[eng].get(sem, 0) >= val:
            return
        self.hwm[eng][sem] = val
        h = self.sems[sem]
        self.prog[eng].append(lambda e, h=h, v=val: e.wait_ge(h, v))

    def op(self, eng, f, waits=(), inc=None, inck=1):
        for ev in waits:
            self.wait(eng, ev)
        if inc is None:
            self.prog[eng].append(f)
            return None
        self.cnt[inc] += inck
        val = self.cnt[inc]
        h = self.sems[inc]
        self.prog[eng].append(lambda e, f=f, h=h, k=inck: f(e).then_inc(h, k))
        return (inc, val)


def build_nc():
    NIC = D // 128
    NMC_Q = HID // 128
    NKC = DFF // 128
    NTILES = NTOK // TILE
    NGRP = TILE // GRP
    NSG = NTOK // SGTOK
    INV_D = 1.0 / D
    NB12 = DFF // 256
    NBH = DFF // 128
    NB3 = D // 128

    nc = bass.Bass("TRN2", target_bir_lowering=False,
                   detect_race_conditions=False)

    xT_d = nc.dram_tensor("xT", [D, NTOK], F32, kind="ExternalInput")
    wq_d = nc.dram_tensor("wq", [D, HID], F32, kind="ExternalInput")
    wk_d = nc.dram_tensor("wk", [D, HID], F32, kind="ExternalInput")
    wv_d = nc.dram_tensor("wv", [D, HID], F32, kind="ExternalInput")
    wo_d = nc.dram_tensor("wo", [HID, D], F32, kind="ExternalInput")
    bocol_d = nc.dram_tensor("bocol", [128, NIC], F32, kind="ExternalInput")
    w1t_d = nc.dram_tensor("w1t", [NB12, 128, NIC * 256], F32, kind="ExternalInput")
    w2t_d = nc.dram_tensor("w2t", [NB12, 128, NIC * 256], F32, kind="ExternalInput")
    h1t_d = nc.dram_tensor("h1t", [NBH, 128, NKC * 128], F32, kind="ExternalInput")
    h2t_d = nc.dram_tensor("h2t", [NBH, 128, NKC * 128], F32, kind="ExternalInput")
    w3t_d = nc.dram_tensor("w3t", [NB3, 128, NKC * 128], F32, kind="ExternalInput")
    w1s_d = nc.dram_tensor("w1s", [128, NKC], F32, kind="ExternalInput")
    w2s_d = nc.dram_tensor("w2s", [128, NKC], F32, kind="ExternalInput")
    h1s_d = nc.dram_tensor("h1s", [128, NKC], F32, kind="ExternalInput")
    h2s_d = nc.dram_tensor("h2s", [128, NKC], F32, kind="ExternalInput")
    w3s_d = nc.dram_tensor("w3s", [128, NIC], F32, kind="ExternalInput")
    qbc_d = nc.dram_tensor("qbc", [2, 128], F32, kind="ExternalInput")
    bones_d = nc.dram_tensor("bones", [128, 2], BF16, kind="ExternalInput")
    onescol_d = nc.dram_tensor("onescol", [128, 1], BF16, kind="ExternalInput")
    onesrow_d = nc.dram_tensor("onesrow", [1, 128], F32, kind="ExternalInput")
    miscc_d = nc.dram_tensor("miscc", [1, 3], F32, kind="ExternalInput")
    onesrowb_d = nc.dram_tensor("onesrowb", [1, 128], BF16, kind="ExternalInput")
    rows_d = nc.dram_tensor("rowsv", [2, NTOK], F32)
    w1b_d = nc.dram_tensor("w1b", [NB12, 128, NIC * 256], BF16)
    w2b_d = nc.dram_tensor("w2b", [NB12, 128, NIC * 256], BF16)
    h1b_d = nc.dram_tensor("h1b", [NBH, 128, NKC * 128], BF16)
    h2b_d = nc.dram_tensor("h2b", [NBH, 128, NKC * 128], BF16)
    w3b_d = nc.dram_tensor("w3b", [NB3, 128, NKC * 128], BF16)
    outT_d = nc.dram_tensor("outT", [D, NTOK], F32, kind="ExternalOutput")

    from contextlib import ExitStack
    with ExitStack() as st:
        en = st.enter_context
        wq_sb = en(nc.sbuf_tensor("s_wq", [128, NIC * HID], BF16))
        m_sb = en(nc.sbuf_tensor("s_m", [128, NMC_Q * D], BF16))
        x2b = en(nc.sbuf_tensor("s_x2b", [128, NIC * SGTOK], BF16))
        h0a = en(nc.sbuf_tensor("s_h0a", [128, NKC * SGTOK], BF16))
        g1a = en(nc.sbuf_tensor("s_g1a", [128, NKC * SGTOK], BF16))
        wringA = en(nc.sbuf_tensor("s_wrA", [128, NKC * 128], BF16))
        wringB = en(nc.sbuf_tensor("s_wrB", [128, NKC * 128], BF16))
        xb0 = en(nc.sbuf_tensor("s_xb0", [128, NIC * TILE], BF16))
        xb1 = en(nc.sbuf_tensor("s_xb1", [128, NIC * TILE], BF16))
        silu_b = [en(nc.sbuf_tensor(f"s_silu{i}", [128, TILE], BF16)) for i in range(2)]
        scr0 = en(nc.sbuf_tensor("s_scr0", [128, TILE], F32))
        scr1 = en(nc.sbuf_tensor("s_scr1", [128, TILE], F32))
        r0 = en(nc.sbuf_tensor("s_r0", [1, TILE], F32))
        r1 = en(nc.sbuf_tensor("s_r1", [1, TILE], F32))
        r2 = en(nc.sbuf_tensor("s_r2", [1, TILE], F32))
        rzc = en(nc.sbuf_tensor("s_rzc", [128, 4], F32))
        bocol = en(nc.sbuf_tensor("s_bocol", [128, NIC], F32))
        w1s = en(nc.sbuf_tensor("s_w1s", [128, NKC], F32))
        w2s = en(nc.sbuf_tensor("s_w2s", [128, NKC], F32))
        h1s = en(nc.sbuf_tensor("s_h1s", [128, NKC], F32))
        h2s = en(nc.sbuf_tensor("s_h2s", [128, NKC], F32))
        w3s = en(nc.sbuf_tensor("s_w3s", [128, NIC], F32))
        qbc = en(nc.sbuf_tensor("s_qbc", [2, 128], F32))
        bones = en(nc.sbuf_tensor("s_bones", [128, 2], BF16))
        onescol = en(nc.sbuf_tensor("s_onescol", [128, 1], BF16))
        onesrow = en(nc.sbuf_tensor("s_onesrow", [1, 128], F32))
        miscc = en(nc.sbuf_tensor("s_miscc", [1, 3], F32))
        onesrow_b = en(nc.sbuf_tensor("s_onesrowb", [1, 128], BF16))
        nc.const_aps.register(F32, EPS, miscc[0:1, 0:1])
        nc.const_aps.register(F32, -0.5, miscc[0:1, 1:2])
        nc.const_aps.register(F32, -1.0, miscc[0:1, 2:3])

        ln1_v = h0a[:, 0:NIC * TILE]
        a_v = h0a[:, NIC * TILE:2 * NIC * TILE]
        sq_v = h0a[:, 2 * NIC * TILE:3 * NIC * TILE]
        expq_v = h0a[:, 3 * NIC * TILE:3 * NIC * TILE + NMC_Q * TILE]
        softq_v = h0a[:, 3 * NIC * TILE + NMC_Q * TILE:
                      3 * NIC * TILE + 2 * NMC_Q * TILE]
        off = 3 * NIC * TILE + 2 * NMC_Q * TILE
        expk_v = h0a[:, off:off + TILE]
        vb_v = h0a[:, off + TILE:off + 2 * TILE]
        ssb_v = h0a[:, off + 2 * TILE:off + 2 * TILE + 4 * 128]
        ssb_accf = h0a[:, 18432:19456].bitcast(F32)
        zk_accf = h0a[:, 19456:20480].bitcast(F32)
        lnp_v = [ln1_v, h0a[:, 20480:20480 + NIC * TILE]]
        sqp_v = [sq_v, h0a[:, 24576:24576 + NIC * TILE]]
        a2_v = ln1_v
        rsbc_b = h0a[:, 28672:29184]
        mrsbc_b = h0a[:, 29184:29696]
        r1b = h0a[0:1, 29696:30208]
        r2b = h0a[0:1, 30208:30720]
        scrb = h0a[:, 30720:31232]
        scrb2 = h0a[:, 31232:31744]
        rzq = h0a[:, 31744:32768].bitcast(F32)[0:2, :]
        ln3_v = g1a[:, 0:NIC * SGTOK]
        wk_v = g1a[:, NIC * SGTOK:NIC * SGTOK + NIC * HID]
        wv_v = g1a[:, NIC * SGTOK + NIC * HID:NIC * SGTOK + 2 * NIC * HID]
        wo_v = g1a[:, NIC * SGTOK + 2 * NIC * HID:
                   NIC * SGTOK + 2 * NIC * HID + NMC_Q * D]
        fo = NIC * SGTOK + 2 * NIC * HID + NMC_Q * D
        f_v = g1a[:, fo:fo + NIC * SGTOK]
        sq4_v = g1a[:, fo + NIC * SGTOK:fo + NIC * SGTOK + NIC * TILE]
        out_sbs = [g1a[:, 0:2 * NIC * TILE].bitcast(F32),
                   g1a[:, fo:fo + 2 * NIC * TILE].bitcast(F32)]
        rowsA_pre = g1a[:, NIC * SGTOK:NIC * SGTOK + 2048].bitcast(F32)
        rowsB_pre = g1a[:, NIC * SGTOK + 2048:NIC * SGTOK + 4096].bitcast(F32)
        wo0 = NIC * SGTOK + 2 * NIC * HID
        bcsets = [
            (rsbc_b, mrsbc_b, r1b, r2b),
            (h0a[:, 12288:12800], h0a[:, 12800:13312],
             h0a[0:1, 13312:13824], h0a[0:1, 13824:14336]),
            (g1a[:, wo0:wo0 + 512], g1a[:, wo0 + 512:wo0 + 1024],
             g1a[0:1, wo0 + 1024:wo0 + 1536], g1a[0:1, wo0 + 1536:wo0 + 2048]),
        ]
        expq2_v = h0a[:, 24576:24576 + NMC_Q * TILE]
        softq2_v = h0a[:, 24576 + NMC_Q * TILE:24576 + 2 * NMC_Q * TILE]

        pb = [en(nc.psum_tensor(f"pb{i}", [128, 512], F32)) for i in range(8)]
        sems = {s: en(nc.semaphore(s)) for s in ("tq", "tg", "tt", "tv", "ts", "to")}
        B = Build(sems)
        bank_w = [None] * 8
        bank_r = [[] for _ in range(8)]

        def mm_group(bank, mms):
            n = len(mms)
            ev = None
            for i, (o, l, r, w) in enumerate(mms):
                ws = list(w)
                if i == 0:
                    ws += bank_r[bank]
                ev = B.op("tensor",
                          lambda e, o=o, l=l, r=r, s=(i == 0), t=(i == n - 1):
                          e.matmul(o, l, r, start=s, stop=t),
                          waits=ws, inc="tt" if i == n - 1 else None)
            bank_w[bank] = ev
            bank_r[bank] = []
            return ev

        def bank_read(eng, bank, f, waits=()):
            inc = {"vector": "tv", "scalar": "ts"}[eng]
            ev = B.op(eng, f, waits=list(waits) + [bank_w[bank]], inc=inc)
            bank_r[bank].append(ev)
            return ev

        def cdma(out_ap, in_ap, waits=()):
            return B.op("gpsimd", lambda e, o=out_ap, i=in_ap: e.dma_start(o, i),
                        waits=list(waits) + [("tg", B.cnt["tg"])],
                        inc="tg", inck=16)

        def qdma(out_ap, in_ap, waits=(), inc="tq"):
            return B.op("sync", lambda e, o=out_ap, i=in_ap: e.dma_start(o, i),
                        waits=list(waits) + [(inc, B.cnt[inc])], inc=inc, inck=16)

        def part3(dram_ap):
            return dram_ap.rearrange("(a p) c -> p a c", p=128)

        def sb3(sb_ap, c):
            return sb_ap.rearrange("p (a c) -> p a c", c=c)

        ev_wq = cdma(sb3(wq_sb[:, :], HID), part3(wq_d.ap()))
        ev_wk = cdma(sb3(wk_v, HID), part3(wk_d.ap()))
        ev_wv = cdma(sb3(wv_v, HID), part3(wv_d.ap()))
        ev_wo = cdma(sb3(wo_v, D), part3(wo_d.ap()))
        ev_consts = None
        for sb, d in ((bocol, bocol_d), (w1s, w1s_d), (w2s, w2s_d), (h1s, h1s_d),
                      (h2s, h2s_d), (w3s, w3s_d), (qbc, qbc_d), (bones, bones_d),
                      (onescol, onescol_d), (onesrow, onesrow_d),
                      (miscc, miscc_d), (onesrow_b, onesrowb_d)):
            ev_consts = qdma(sb[:, :], d.ap())

        xbuf = [xb0, xb1]
        xb_ready = [None, None]
        xb_users = [[], []]

        def load_xb(t):
            p = t % 2
            src = part3(xT_d.ap()[:, t * TILE:(t + 1) * TILE])
            xb_ready[p] = cdma(sb3(xbuf[p][:, :], TILE), src,
                               waits=list(xb_users[p]))
            xb_users[p] = []

        rows_war = []
        rowsb_war = [[], [], []]
        bc_idx = [0]

        def ln_stats(src_fn, sq_fn, src_ev, bsum, bsq, sq_eng="scalar",
                     src3=None, sq3=None, sq_extra=()):
            nonlocal rows_war
            sq_w = [src_ev(ic) for ic in range(NIC)] + list(sq_extra)
            if sq_eng == "scalar":
                ev_sq = B.op("scalar", lambda e: e.square(sq3, src3),
                             waits=sq_w, inc="ts")
            else:
                ev_sq = B.op("vector", lambda e: e.tensor_mul(sq3, src3, src3),
                             waits=sq_w, inc="tv")
            mm_group(bsum, [(pb[bsum][0:1, :], onescol[:, :], src_fn(ic),
                             [src_ev(ic), ev_consts]) for ic in range(NIC)])
            mm_group(bsq, [(pb[bsq][0:1, :], onescol[:, :], sq_fn(ic),
                            [ev_sq]) for ic in range(NIC)])
            war = list(rows_war)
            rows_war = []
            bank_read("vector", bsum,
                      lambda e, b=bsum: e.tensor_scalar_mul(r0[:, :], pb[b][0:1, :],
                                                            INV_D),
                      waits=war)
            bank_read("vector", bsq,
                      lambda e, b=bsq: e.tensor_scalar_mul(r1[:, :], pb[b][0:1, :],
                                                           INV_D))
            B.op("vector", lambda e: e.tensor_mul(r2[:, :], r0[:, :], r0[:, :]))
            ev_var = B.op("vector",
                          lambda e: e.tensor_sub(r1[:, :], r1[:, :], r2[:, :]),
                          inc="tv")
            ev_ln = B.op("scalar",
                         lambda e: e.activation(r2[:, :], r1[:, :], AF.Ln, bias=EPS),
                         waits=[ev_var, ev_consts], inc="ts")
            ev_rs = B.op("scalar",
                         lambda e: e.activation(r1[:, :], r2[:, :], AF.Exp, scale=-0.5),
                         waits=[ev_ln], inc="ts")
            return B.op("vector", lambda e: e.tensor_mul(r2[:, :], r0[:, :], r1[:, :]),
                        waits=[ev_rs], inc="tv")

        def ln_bc32(ev_rows, b_rs, b_mrs):
            e1 = mm_group(b_rs, [(pb[b_rs][:, :], onesrow[:, :], r1[:, :],
                                  [ev_rows, ev_consts])])
            e2 = mm_group(b_mrs, [(pb[b_mrs][:, :], onesrow[:, :], r2[:, :], [])])
            rows_war.extend([e1, e2])

        def ln_bc(ev_rows, b_rs, b_mrs, rowsrc=None, si=None, extra_bc=()):
            if si is None:
                si = bc_idx[0] = 1 - bc_idx[0]
            rsb, mrsb, c1b, c2b = bcsets[si]
            ra, rb = rowsrc if rowsrc is not None else (r1[:, :], r2[:, :])
            wb = list(rowsb_war[si]) + list(extra_bc)
            rowsb_war[si] = []
            ev_c1 = B.op("vector", lambda e, ra=ra, d=c1b: e.tensor_copy(d, ra),
                         waits=[ev_rows] + wb, inc="tv")
            ev_c2 = B.op("vector", lambda e, rb=rb, d=c2b: e.tensor_copy(d, rb),
                         inc="tv")
            rows_war.append(ev_c2)
            e1 = mm_group(b_rs, [(pb[b_rs][:, :], onesrow_b[:, :], c1b,
                                  [ev_c1, ev_consts])])
            e2 = mm_group(b_mrs, [(pb[b_mrs][:, :], onesrow_b[:, :], c2b,
                                   [ev_c2])])
            rowsb_war[si].extend([e1, e2])
            bank_read("vector", b_rs,
                      lambda e, b=b_rs, d=rsb: e.tensor_copy(d, pb[b][:, :]))
            ev = bank_read("vector", b_mrs,
                           lambda e, b=b_mrs, d=mrsb: e.tensor_copy(d, pb[b][:, :]))
            return (ev, si)

        def ln_apply32(src_fn, dst_fn, b_rs, b_mrs, src_ev, extra_fn=None,
                       first_wait=None):
            ev = None
            for ic in range(NIC):
                s_ap, d_ap = src_fn(ic), dst_fn(ic)
                w = [src_ev(ic)]
                if ic == 0 and first_wait is not None:
                    w.append(first_wait)
                bank_read("vector", b_rs,
                          lambda e, s=s_ap, b=b_rs: e.tensor_mul(scr0[:, :], s,
                                                                 pb[b][:, :]),
                          waits=w)
                if extra_fn is None:
                    ev = bank_read("vector", b_mrs,
                                   lambda e, d=d_ap, b=b_mrs: e.tensor_sub(
                                       d, scr0[:, :], pb[b][:, :]))
                else:
                    x_ap = extra_fn(ic)
                    bank_read("vector", b_mrs,
                              lambda e, b=b_mrs: e.tensor_sub(scr1[:, :], scr0[:, :],
                                                              pb[b][:, :]))
                    ev = B.op("vector",
                              lambda e, d=d_ap, x=x_ap: e.tensor_add(d, scr1[:, :], x),
                              inc="tv")
            return ev

        def ln_apply(src3, dst3, ev_bc, ev_srcs, extra3=None, first_wait=None,
                     tmp3=None):
            ev_bc, si = ev_bc
            nchunk = src3.shape[1]
            rs3 = bcsets[si][0].unsqueeze(1).broadcast_to((128, nchunk, TILE))
            mrs3 = bcsets[si][1].unsqueeze(1).broadcast_to((128, nchunk, TILE))
            w = list(ev_srcs) + [ev_bc]
            if first_wait is not None:
                w.append(first_wait)
            t3 = dst3 if tmp3 is None else tmp3
            B.op("vector", lambda e: e.tensor_mul(t3, src3, rs3), waits=w)
            if extra3 is None:
                return B.op("vector", lambda e: e.tensor_sub(dst3, t3, mrs3),
                            inc="tv")
            B.op("vector", lambda e: e.tensor_sub(t3, t3, mrs3))
            return B.op("vector", lambda e: e.tensor_add(dst3, t3, extra3),
                        inc="tv")

        def c3(ap, n=TILE):
            return ap.rearrange("p (c n) -> p c n", n=n)

        def sg3(ap, nt):
            return ap.rearrange("p (c g) -> p c g", g=SGTOK)[:, :, nt * TILE:(nt + 1) * TILE]


        slots = {"A": {"ap": wringA, "users": []},
                 "B": {"ap": wringB, "users": []}}

        bmap = {id(w1t_d): w1b_d, id(w2t_d): w2b_d, id(h1t_d): h1b_d,
                id(h2t_d): h2b_d, id(w3t_d): w3b_d}
        pp_state = {"done": {"tq": None, "to": None}, "n": 0}
        pp_written = set()

        def wload(sl, dram, blk, ncols, half=None, sg=1):
            s = slots[sl]
            dst = s["ap"][:, 0:ncols] if half is None else \
                s["ap"][:, half * ncols:(half + 1) * ncols]
            if sg < 2:
                src_ap = dram.ap()[blk, :, 0:ncols]
            else:
                src_ap = bmap[id(dram)].ap()[blk, :, 0:ncols]
            w = list(s["users"])
            if sg >= 2:
                w += [pp_state["done"]["tq"], pp_state["done"]["to"]]
            ev = cdma(dst, src_ap, waits=w)
            s["users"] = []
            if sg == 0 and (id(dram), blk) not in pp_written:
                pp_written.add((id(dram), blk))
                q = ("tq", "to")[pp_state["n"] % 2]
                pp_state["n"] += 1
                ev_s = qdma(bmap[id(dram)].ap()[blk, :, 0:ncols], dst,
                            waits=[ev], inc=q)
                s["users"].append(ev_s)
                pp_state["done"][q] = ev_s
            return ev


        ev_init = B.op("vector", lambda e: e.memset(ssb_accf[:, :], 0.0), inc="tv")
        ev_init2 = B.op("vector", lambda e: e.memset(zk_accf[0:1, :], 0.0), inc="tv")

        def p1_chain_stats(t):
            p4 = 4 * (t % 2)
            xb = xbuf[t % 2]
            xev = xb_ready[t % 2]
            return ln_stats(lambda i: xb[:, i * TILE:(i + 1) * TILE],
                            lambda i: sqp_v[t % 2][:, i * TILE:(i + 1) * TILE],
                            lambda i, ev=xev: ev, p4, p4 + 1,
                            src3=c3(xb[:, :]), sq3=c3(sqp_v[t % 2]))

        def p1_chain_bc(t, ev_rows):
            p4 = 4 * (t % 2)
            sl = slice(t * TILE, (t + 1) * TILE)
            e1 = qdma(rows_d.ap()[0:1, sl], r1[:, :], waits=[ev_rows])
            e2 = qdma(rows_d.ap()[1:2, sl], r2[:, :])
            rows_war.append(e2)
            return ln_bc(ev_rows, p4 + 2, p4 + 3)

        def p1_apply_mul(t, ev_bc):
            ev_bc, si = ev_bc
            xb = xbuf[t % 2]
            rs3 = bcsets[si][0].unsqueeze(1).broadcast_to((128, NIC, TILE))
            B.op("vector",
                 lambda e: e.tensor_mul(c3(lnp_v[t % 2]), c3(xb[:, :]), rs3),
                 waits=[xb_ready[t % 2], ev_bc])
            return si

        def p1_apply_sub(t, si):
            mrs3 = bcsets[si][1].unsqueeze(1).broadcast_to((128, NIC, TILE))
            ev = B.op("vector",
                      lambda e: e.tensor_sub(c3(lnp_v[t % 2]), c3(lnp_v[t % 2]),
                                             mrs3),
                      inc="tv")
            xb_users[t % 2].append(ev)
            return ev

        def p1_groups(t, g, ev_ln1):
            p4 = 4 * (t % 2)
            lnv = lnp_v[t % 2]
            bk, bv, bs, bz = p4, p4 + 1, p4 + 2, p4 + 3
            mm_group(bk, [
                (pb[bk][:, :],
                 lnv[:, ic * TILE + g * GRP: ic * TILE + (g + 1) * GRP],
                 wk_v[:, ic * HID:(ic + 1) * HID],
                 [ev_ln1, ev_wk] if ic == 0 else ()) for ic in range(NIC)])
            mm_group(bv, [
                (pb[bv][:, :],
                 lnv[:, ic * TILE + g * GRP: ic * TILE + (g + 1) * GRP],
                 wv_v[:, ic * HID:(ic + 1) * HID],
                 [ev_ln1, ev_wv] if ic == 0 else ()) for ic in range(NIC)])
            ev_ek = bank_read("scalar", bk,
                              lambda e, b=bk: e.activation(expk_v, pb[b][:, :],
                                                           AF.Exp))
            ev_vb = bank_read("scalar", bv,
                              lambda e, b=bv: e.copy(vb_v, pb[b][:, :]))
            for j in range(4):
                mm_group(bs, [(pb[bs][:, j * 128:(j + 1) * 128],
                               vb_v[:, j * 128:(j + 1) * 128],
                               expk_v[:, j * 128:(j + 1) * 128],
                               [ev_vb, ev_ek])])
            bank_read("vector", bs,
                      lambda e, b=bs: e.tensor_add(ssb_accf[:, :], ssb_accf[:, :],
                                                   pb[b][:, :]),
                      waits=[ev_init])
            mm_group(bz, [(pb[bz][0:1, :], onescol[:, :], expk_v[:, :],
                           [ev_ek, ev_consts])])
            bank_read("vector", bz,
                      lambda e, b=bz: e.tensor_add(zk_accf[0:1, :], zk_accf[0:1, :],
                                                   pb[b][0:1, :]),
                      waits=[ev_init2])

        load_xb(0)
        if NTILES > 1:
            load_xb(1)
        ev_rows_n = p1_chain_stats(0)
        ev_bc_n = p1_chain_bc(0, ev_rows_n)
        si_n = p1_apply_mul(0, ev_bc_n)
        ev_ln1_cur = p1_apply_sub(0, si_n)
        for t in range(NTILES):
            nxt = t + 1 < NTILES
            p1_groups(t, 0, ev_ln1_cur)
            if nxt:
                ev_rows_n = p1_chain_stats(t + 1)
            p1_groups(t, 1, ev_ln1_cur)
            if t + 2 < NTILES:
                load_xb(t + 2)
            if nxt:
                ev_bc_n = p1_chain_bc(t + 1, ev_rows_n)
            p1_groups(t, 2, ev_ln1_cur)
            if nxt:
                si_n = p1_apply_mul(t + 1, ev_bc_n)
            p1_groups(t, 3, ev_ln1_cur)
            if nxt:
                ev_ln1_cur = p1_apply_sub(t + 1, si_n)

        ev_ssb = []
        for j in range(4):
            ev_ssb.append(B.op("vector",
                               lambda e, j=j: e.tensor_copy(
                                   ssb_v[:, j * 128:(j + 1) * 128],
                                   ssb_accf[:, j * 128:(j + 1) * 128]),
                               inc="tv"))
        for ev in rows_war:
            B.wait("vector", ev)
        rows_war = []
        ev_rzk = B.op("vector", lambda e: e.reciprocal(r1[:, :], zk_accf[0:1, :]),
                      inc="tv")
        ev_rz = None
        for j in range(4):
            ev_rz = B.op("tensor",
                         lambda e, j=j: e.matmul(pb[5][:, j:j + 1],
                                                 r1[0:1, j * 128:(j + 1) * 128],
                                                 onesrow[0:1, 0:1],
                                                 start=True, stop=True),
                         waits=[ev_rzk, ev_consts] + (bank_r[5] if j == 0 else []),
                         inc="tt" if j == 3 else None)
        bank_w[5] = ev_rz
        bank_r[5] = []
        ev_rzc = bank_read("vector", 5,
                           lambda e: e.tensor_scalar_mul(rzc[:, :], pb[5][:, 0:4],
                                                         SCALE))
        rows_war.append(ev_rz)
        ev_m_done = None
        for j in range(4):
            for half in range(2):
                mm_group(6, [(pb[6][0:64, :], ssb_v[0:64, j * 128:j * 128 + 64],
                              wo_v[0:64, j * D + half * 512:j * D + (half + 1) * 512],
                              [ev_ssb[j], ev_wo])])
                mm_group(7, [(pb[7][64:128, :],
                              ssb_v[64:128, j * 128 + 64:j * 128 + 128],
                              wo_v[64:128, j * D + half * 512:j * D + (half + 1) * 512],
                              [ev_ssb[j], ev_wo])])
                bank_read("vector", 6,
                          lambda e, j=j, h=half: e.tensor_scalar_mul(
                              m_sb[0:64, j * D + h * 512:j * D + (h + 1) * 512],
                              pb[6][0:64, :], rzc[0:64, j:j + 1]),
                          waits=[ev_rzc])
                ev_m_done = bank_read("vector", 7,
                                      lambda e, j=j, h=half: e.tensor_scalar_mul(
                                          m_sb[64:128, j * D + h * 512:
                                               j * D + (h + 1) * 512],
                                          pb[7][64:128, :], rzc[64:128, j:j + 1]),
                                      waits=[ev_rzc])

        ev_od = [None, None]
        rz_war = None
        silu_war = [None, None]
        rows_pre_users = [[], []]

        rows_pre_ev = [None, None]

        def preload_rows(t):
            par = t % 2
            buf = rowsA_pre if par == 0 else rowsB_pre
            sl = slice(t * TILE, (t + 1) * TILE)
            qdma(buf[0:1, 0:TILE], rows_d.ap()[0:1, sl],
                 waits=list(rows_pre_users[par]))
            rows_pre_ev[par] = qdma(buf[0:1, TILE:2 * TILE], rows_d.ap()[1:2, sl])
            rows_pre_users[par] = []

        def a_ln1(t, extra=(), extra_bc=()):
            par = t % 2
            buf = rowsA_pre if par == 0 else rowsB_pre
            xev = xb_ready[par]
            xb = xbuf[par]
            ev_bc = ln_bc(rows_pre_ev[par], 6, 7,
                          rowsrc=(buf[0:1, 0:TILE], buf[0:1, TILE:2 * TILE]),
                          si=2, extra_bc=extra_bc)
            rows_pre_users[par].append(ev_bc[0])
            return ln_apply(c3(xb[:, :]), c3(lnp_v[par]), ev_bc,
                            [xev] + list(extra))

        def a_q(t, ev_ln1, pre=False):
            nonlocal rz_war
            exq = expq2_v if pre else expq_v
            sfq = softq2_v if pre else softq_v
            lnv = lnp_v[t % 2]
            ev_expq = [None] * NMC_Q
            ev_softq = [None] * NMC_Q
            for mc in range(NMC_Q):
                mm_group(mc, [(pb[mc][:, :],
                               wq_sb[:, ic * HID + mc * 128:ic * HID + (mc + 1) * 128],
                               lnv[:, ic * TILE:(ic + 1) * TILE],
                               [ev_ln1, ev_wq] if ic == 0 else ())
                              for ic in range(NIC)])
                ev_expq[mc] = bank_read(
                    "scalar", mc,
                    lambda e, m=mc, x=exq: e.activation(
                        x[:, m * TILE:(m + 1) * TILE], pb[m][:, :], AF.Exp))
            for mc in range(NMC_Q):
                mm_group(mc, [(pb[mc][0:2, :], bones[:, :],
                               exq[:, mc * TILE:(mc + 1) * TILE],
                               [ev_expq[mc], ev_consts])])
            for mc in range(NMC_Q):
                ev_lnz = bank_read("scalar", mc,
                                   lambda e, m=mc: e.activation(rzq[:, :],
                                                                pb[m][0:2, :], AF.Ln),
                                   waits=[rz_war] if rz_war else ())
                rz_war = mm_group(mc, [(pb[mc][:, :], qbc[:, :], rzq[:, :],
                                        [ev_lnz, ev_consts])])
                dstb = (6 + mc % 2) if pre else (4 + mc % 2)
                ev_bcz = bank_w[mc]
                ev_ez = B.op("scalar",
                             lambda e, m=mc, d=dstb: e.activation(
                                 pb[d][:, :], pb[m][:, :], AF.Exp, scale=-1.0),
                             waits=[ev_bcz] + bank_r[dstb] +
                             ([bank_w[dstb]] if bank_w[dstb] else []),
                             inc="ts")
                bank_r[mc].append(ev_ez)
                bank_w[dstb] = ev_ez
                bank_r[dstb] = []
                ev_softq[mc] = bank_read(
                    "vector", dstb,
                    lambda e, m=mc, d=dstb, x=exq, s=sfq: e.tensor_mul(
                        s[:, m * TILE:(m + 1) * TILE],
                        x[:, m * TILE:(m + 1) * TILE], pb[d][:, :]))
            return ev_softq

        def a_attn(ev_softq, a_dst, ocs=None, ev_a=None, sfq=None):
            if sfq is None:
                sfq = softq_v
            if ev_a is None:
                ev_a = [None] * NIC
            for oc in (ocs if ocs is not None else range(NIC)):
                ab = 4 + (oc % 2)
                mm_group(ab, [(pb[ab][:, :],
                               m_sb[:, mc * D + oc * 128:mc * D + (oc + 1) * 128],
                               sfq[:, mc * TILE:(mc + 1) * TILE],
                               [ev_softq[mc]] + ([ev_m_done] if oc == 0 else []))
                              for mc in range(NMC_Q)])
                if oc % 2 == 0:
                    ev_a[oc] = bank_read(
                        "vector", ab,
                        lambda e, o=oc, b=ab, d=a_dst: e.tensor_scalar_add(
                            d[:, o * TILE:(o + 1) * TILE], pb[b][:, :],
                            bocol[:, o:o + 1]),
                        waits=[ev_consts])
                else:
                    ev_a[oc] = bank_read(
                        "scalar", ab,
                        lambda e, o=oc, b=ab, d=a_dst: e.add(
                            d[:, o * TILE:(o + 1) * TILE], pb[b][:, :],
                            bocol[:, o:o + 1]),
                        waits=[ev_consts])
            return ev_a

        class LN23:

            def __init__(self, t, nt, a_src, ev_a, sq_extra=()):
                self.t, self.nt, self.a_src, self.ev_a = t, nt, a_src, ev_a
                self.sq_extra = sq_extra

            def p1(self):
                t, nt = self.t, self.nt
                self.ev_rows2 = ln_stats(
                    lambda i: self.a_src[:, i * TILE:(i + 1) * TILE],
                    lambda i: sqp_v[t % 2][:, i * TILE:(i + 1) * TILE],
                    lambda i: self.ev_a[i], 6, 7,
                    src3=c3(self.a_src), sq3=c3(sqp_v[t % 2]),
                    sq_extra=self.sq_extra)

            def p2(self):
                t, nt = self.t, self.nt
                xb = xbuf[t % 2]
                ev_bc2 = ln_bc(self.ev_rows2, 6, 7)
                self.ev_x2 = ln_apply(c3(self.a_src), sg3(x2b[:, :], nt), ev_bc2,
                                      self.ev_a, extra3=c3(xb[:, :]))
                xb_users[t % 2].append(self.ev_x2)

            def p3(self):
                t, nt = self.t, self.nt
                self.ev_rows3 = ln_stats(
                    lambda i: x2b[:, i * SGTOK + nt * TILE:
                                  i * SGTOK + (nt + 1) * TILE],
                    lambda i: sqp_v[t % 2][:, i * TILE:(i + 1) * TILE],
                    lambda i, ev=self.ev_x2: ev, 6, 7,
                    src3=sg3(x2b[:, :], nt), sq3=c3(sqp_v[t % 2]))

            def p4(self):
                nt = self.nt
                ev_bc3 = ln_bc(self.ev_rows3, 6, 7)
                self.ev_ln3 = ln_apply(sg3(x2b[:, :], nt), sg3(ln3_v, nt),
                                       ev_bc3, [self.ev_x2],
                                       first_wait=ev_od[0])
                return self.ev_ln3

        rows_pre_users[0].append(ev_m_done)
        rows_pre_users[1].append(ev_m_done)
        preload_rows(0)
        if NTILES > 1:
            preload_rows(1)
        a2B_v = lnp_v[1]
        pref = {}
        ev_oap = [None, None]
        for sg in range(NSG):
            tA, tB = sg * 2, sg * 2 + 1
            if tA in pref:
                sqA, sfqA = pref.pop(tA)
                load_xb(tB)
                evB_ln1 = a_ln1(tB)
                evA_a = a_attn(sqA, a_v, sfq=sfqA)
                sqB = a_q(tB, evB_ln1)
            else:
                load_xb(tA)
                load_xb(tB)
                evA_ln1 = a_ln1(tA, extra_bc=[ev_m_done])
                sqA = a_q(tA, evA_ln1)
                evB_ln1 = a_ln1(tB, extra_bc=[ev_m_done])
                evA_a = a_attn(sqA, a_v)
                sqB = a_q(tB, evB_ln1)
            chA = LN23(tA, 0, a_v, evA_a, sq_extra=(
                [ev_oap[1]] if ev_oap[1] else []))
            chA.p1()
            evB_a = a_attn(sqB, a2B_v, ocs=range(0, 4))
            chA.p2()
            a_attn(sqB, a2B_v, ocs=range(4, NIC), ev_a=evB_a)
            chA.p3()
            ev_ln3_nt = [chA.p4(), None]
            chB = LN23(tB, 1, a2B_v, evB_a)
            w12_hooks = [chB.p1, chB.p2, chB.p3,
                         lambda: ev_ln3_nt.__setitem__(1, chB.p4())]

            ev_h0 = None
            ev_w1r = [None, None]
            ev_w2r = [None, None]
            ev_w1r[0] = wload("A", w1t_d, 0, NIC * 256, half=0, sg=sg)
            ev_w2r[0] = wload("B", w2t_d, 0, NIC * 256, half=0, sg=sg)
            if NB12 > 1:
                ev_w1r[1] = wload("A", w1t_d, 1, NIC * 256, half=1, sg=sg)
                ev_w2r[1] = wload("B", w2t_d, 1, NIC * 256, half=1, sg=sg)
            sched = [(0, (0,)), (1, (0,))] + \
                    [(bb, (0, 1)) for bb in range(2, NB12)] + \
                    [(0, (1,)), (1, (1,))]
            it = 0
            for pos, (b, nts) in enumerate(sched):
                hb = pos % 2
                w1ap = wringA[:, hb * NIC * 256:(hb + 1) * NIC * 256]
                w2ap = wringB[:, hb * NIC * 256:(hb + 1) * NIC * 256]
                ev_last_use = None
                for nt in nts:
                    for j in range(2):
                        mc = 2 * b + j
                        b1, b2 = [(0, 1), (2, 3), (4, 5)][it % 3]
                        mm_group(b1, [(pb[b1][:, :],
                                       w1ap[:, ic * 256 + j * 128:
                                            ic * 256 + (j + 1) * 128],
                                       ln3_v[:, ic * SGTOK + nt * TILE:
                                             ic * SGTOK + (nt + 1) * TILE],
                                       [ev_w1r[hb], ev_ln3_nt[nt]]
                                       if ic == 0 else ())
                                      for ic in range(NIC)])
                        ev_last_use = mm_group(
                            b2, [(pb[b2][:, :],
                                  w2ap[:, ic * 256 + j * 128:
                                       ic * 256 + (j + 1) * 128],
                                  ln3_v[:, ic * SGTOK + nt * TILE:
                                        ic * SGTOK + (nt + 1) * TILE],
                                  [ev_w2r[hb]] if ic == 0 else ())
                                 for ic in range(NIC)])
                        sp = silu_b[it % 2]
                        ev_si = bank_read(
                            "scalar", b1,
                            lambda e, m=mc, s=sp, b_=b1: e.activation(
                                s[:, :], pb[b_][:, :], AF.Silu,
                                scale=w1s[:, m:m + 1]),
                            waits=[ev_consts, silu_war[it % 2]]
                            if silu_war[it % 2] else [ev_consts])
                        ev_h0 = bank_read(
                            "vector", b2,
                            lambda e, m=mc, n=nt, s=sp, b_=b2:
                            e.scalar_tensor_tensor(
                                h0a[:, m * SGTOK + n * TILE:
                                    m * SGTOK + (n + 1) * TILE],
                                pb[b_][:, :], w2s[:, m:m + 1], s[:, :],
                                ALU.mult, ALU.mult),
                            waits=[ev_si])
                        silu_war[it % 2] = ev_h0
                        it += 1
                        if w12_hooks:
                            w12_hooks.pop(0)()
                slots["A"]["users"].append(ev_last_use)
                slots["B"]["users"].append(ev_last_use)
                if pos + 2 < len(sched):
                    nb = sched[pos + 2][0]
                    ev_w1r[hb] = wload("A", w1t_d, nb, NIC * 256, half=hb, sg=sg)
                    ev_w2r[hb] = wload("B", w2t_d, nb, NIC * 256, half=hb, sg=sg)

            def h_stage(dram, s_col, dst_base, rhs_base, ev_rhs_done):
                ev_done = None
                ev_r = [None, None]
                ev_r[0] = wload("A", dram, 0, NKC * 128, sg=sg)
                if NBH > 1:
                    ev_r[1] = wload("B", dram, 1, NKC * 128, sg=sg)
                for b in range(NBH):
                    sl = "A" if b % 2 == 0 else "B"
                    wap = slots[sl]["ap"]
                    ev_use = None
                    for nt in range(2):
                        bk = 4 + (b * 2 + nt) % 4
                        ev_use = mm_group(bk, [
                            (pb[bk][:, :], wap[:, kc * 128:(kc + 1) * 128],
                             rhs_base[:, kc * SGTOK + nt * TILE:
                                      kc * SGTOK + (nt + 1) * TILE],
                             [ev_r[b % 2], ev_rhs_done] if kc == 0 else ())
                            for kc in range(NKC)])
                        ev_done = bank_read(
                            "scalar", bk,
                            lambda e, m=b, n=nt, b_=bk, dst=dst_base, sc=s_col:
                            e.activation(
                                dst[:, m * SGTOK + n * TILE:m * SGTOK + (n + 1) * TILE],
                                pb[b_][:, :], AF.Gelu, scale=sc[:, m:m + 1]),
                            waits=[ev_consts])
                    slots[sl]["users"].append(ev_use)
                    if b + 2 < NBH:
                        ev_r[b % 2] = wload(sl, dram, b + 2, NKC * 128, sg=sg)
                return ev_done

            ev_g1 = h_stage(h1t_d, h1s, g1a, h0a, ev_h0)
            ev_g2 = h_stage(h2t_d, h2s, h0a, g1a, ev_g1)
            if sg + 1 < NSG:
                rows_pre_users[0].append(ev_g2)
                rows_pre_users[1].append(ev_g2)
                preload_rows(2 * (sg + 1))
                preload_rows(2 * (sg + 1) + 1)

            ev_rw3 = [None, None]
            ev_rw3[0] = wload("A", w3t_d, 0, NKC * 128, sg=sg)
            if NB3 > 1:
                ev_rw3[1] = wload("B", w3t_d, 1, NKC * 128, sg=sg)
            ev_f = [[None] * NIC, [None] * NIC]
            for b in range(NB3):
                oc = b
                sl = "A" if b % 2 == 0 else "B"
                wap = slots[sl]["ap"]
                ev_use = None
                for nt in range(2):
                    bk = 6 + (b * 2 + nt) % 2
                    ev_use = mm_group(bk, [
                        (pb[bk][:, :], wap[:, kc * 128:(kc + 1) * 128],
                         h0a[:, kc * SGTOK + nt * TILE:kc * SGTOK + (nt + 1) * TILE],
                         [ev_rw3[b % 2], ev_g2] if kc == 0 else ())
                        for kc in range(NKC)])
                    eng_ = "vector" if oc % 2 == 0 else "scalar"
                    def f_ev(e, o=oc, n=nt, b_=bk):
                        dst = f_v[:, o * SGTOK + n * TILE:
                                  o * SGTOK + (n + 1) * TILE]
                        if isinstance(e, type(None)):
                            return None
                        try:
                            return e.tensor_scalar_mul(dst, pb[b_][:, :],
                                                       w3s[:, o:o + 1])
                        except AttributeError:
                            return e.mul(dst, pb[b_][:, :], w3s[:, o:o + 1])
                    ev_f[nt][oc] = bank_read(
                        eng_, bk, f_ev,
                        waits=[ev_consts, ev_od[1]] if (b == 0 and nt == 0)
                        else [ev_consts])
                slots[sl]["users"].append(ev_use)
                if b + 2 < NB3:
                    ev_rw3[b % 2] = wload(sl, w3t_d, b + 2, NKC * 128, sg=sg)

            if sg + 1 < NSG:
                tN = 2 * (sg + 1)
                load_xb(tN)
                evn = a_ln1(tN, extra=(ev_f[1][NIC - 1],))
                sqn = a_q(tN, evn, pre=True)
                pref[tN] = (sqn, softq2_v)

            def ln4_stats(nt):
                sqr = sq4_v if nt == 0 else sqp_v[0]
                bs = (4, 5) if nt == 0 else (6, 7)
                return ln_stats(
                    lambda i: f_v[:, i * SGTOK + nt * TILE:
                                  i * SGTOK + (nt + 1) * TILE],
                    lambda i, s=sqr: s[:, i * TILE:(i + 1) * TILE],
                    lambda i: ev_f[nt][i], bs[0], bs[1], sq_eng="vector",
                    src3=sg3(f_v, nt), sq3=c3(sqr))

            def ln4_out(nt, ev_bc4):
                t = sg * 2 + nt
                tmp = sq4_v if nt == 0 else sqp_v[0]
                ev_out = ln_apply(
                    sg3(f_v, nt), c3(out_sbs[nt][:, :]), ev_bc4, ev_f[nt],
                    extra3=sg3(x2b[:, :], nt), first_wait=ev_od[nt],
                    tmp3=c3(tmp))
                ev_oap[nt] = ev_out
                ev_od[nt] = qdma(
                    part3(outT_d.ap()[:, t * TILE:(t + 1) * TILE]),
                    sb3(out_sbs[nt][:, :], TILE), waits=[ev_out], inc="to")

            ev_r40 = ln4_stats(0)
            ev_bc40 = ln_bc(ev_r40, 4, 5)
            ev_r41 = ln4_stats(1)
            ev_bc41 = ln_bc(ev_r41, 6, 7)
            ln4_out(0, ev_bc40)
            ln4_out(1, ev_bc41)

        to_total = B.cnt["to"]
        B.op("sync", lambda e, v=to_total: e.wait_ge(sems["to"], v))

        n_inst = sum(len(v) for v in B.prog.values())
        print("sbuf bytes remaining:", nc.sbuf_bytes_remaining,
              "thunks:", {k: len(v) for k, v in B.prog.items()}, "total", n_inst)

        with nc.Block() as block:
            @block.tensor
            def _(eng):
                for f in B.prog["tensor"]:
                    f(eng)

            @block.vector
            def _(eng):
                for f in B.prog["vector"]:
                    f(eng)

            @block.scalar
            def _(eng):
                for f in B.prog["scalar"]:
                    f(eng)

            @block.sync
            def _(eng):
                for f in B.prog["sync"]:
                    f(eng)

            @block.gpsimd
            def _(eng):
                for f in B.prog["gpsimd"]:
                    f(eng)

    return nc


def host_layout(inputs):
    NIC = D // 128
    NKC = DFF // 128
    x = np.asarray(inputs["L0_attn_x"], dtype=np.float32)

    def col(v, n):
        return np.ascontiguousarray(np.asarray(v, np.float32).reshape(n, 128).T)

    def stream(w, nk, bw):
        wT = np.ascontiguousarray(np.asarray(w, np.float32).T)
        nb = wT.shape[1] // bw
        return np.ascontiguousarray(
            wT.reshape(nk, 128, nb, bw).transpose(2, 1, 0, 3).reshape(nb, 128, nk * bw))

    qbc = np.zeros((2, 128), np.float32)
    qbc[0, 0:64] = 1.0
    qbc[1, 64:128] = 1.0
    bones = np.zeros((128, 2), ml_dtypes.bfloat16)
    bones[0:64, 0] = 1.0
    bones[64:128, 1] = 1.0

    shared = {
        "wq": np.ascontiguousarray(np.asarray(inputs["L0_attn_Wq"], np.float32)),
        "wk": np.ascontiguousarray(np.asarray(inputs["L0_attn_Wk"], np.float32)),
        "wv": np.ascontiguousarray(np.asarray(inputs["L0_attn_Wv"], np.float32)),
        "wo": np.ascontiguousarray(np.asarray(inputs["L0_attn_Wo"], np.float32)),
        "bocol": col(inputs["L0_attn_bo"], NIC),
        "w1t": stream(inputs["L0_ffn_w1_w"], NIC, 256),
        "w2t": stream(inputs["L0_ffn_w2_w"], NIC, 256),
        "h1t": stream(inputs["L0_ffn_h1_w"], NKC, 128),
        "h2t": stream(inputs["L0_ffn_h2_w"], NKC, 128),
        "w3t": stream(inputs["L0_ffn_w3_w"], NKC, 128),
        "w1s": col(inputs["L0_ffn_w1_s"], NKC),
        "w2s": col(inputs["L0_ffn_w2_s"], NKC),
        "h1s": col(inputs["L0_ffn_h1_s"], NKC),
        "h2s": col(inputs["L0_ffn_h2_s"], NKC),
        "w3s": col(inputs["L0_ffn_w3_s"], NIC),
        "qbc": qbc,
        "bones": bones,
        "onescol": np.ones((128, 1), ml_dtypes.bfloat16),
        "onesrow": np.ones((1, 128), np.float32),
        "miscc": np.array([[EPS, -0.5, -1.0]], np.float32),
        "onesrowb": np.ones((1, 128), ml_dtypes.bfloat16),
    }
    return [dict(shared, xT=np.ascontiguousarray(x[b].T)) for b in range(x.shape[0])]


def kernel(**inputs):
    in_maps = host_layout(inputs)
    nc = build_nc()
    res = run_bass_kernel_spmd(nc, in_maps, core_ids=list(range(len(in_maps))))
    out = np.empty((len(in_maps), NTOK, D), np.float32)
    for b in range(len(in_maps)):
        out[b] = res.results[b]["outT"].T
    return out
